# version 01
# baseline (speedup 1.0000x reference)
import numpy as np
import ml_dtypes
from contextlib import ExitStack

import concourse.bass as bass
import concourse.mybir as mybir
from concourse.bass_utils import run_bass_kernel_spmd

F32 = mybir.dt.float32
BF16 = mybir.dt.bfloat16
I32 = mybir.dt.int32
AF = mybir.ActivationFunctionType
ALU = mybir.AluOpType
AX = mybir.AxisListType

NCORES = 8
D = 4096
T = 8192
TC = T // NCORES
TH = TC // 2
ND = D // 128
HID = 11008
NF = HID // 128
FBLK = 12
NBLK = (NF + FBLK - 1) // FBLK
R = 16
NHEAD = 8
EH = 64
DSTATE = 16
DTRANK = 128
DLOC = D // NCORES
NDL = DLOC // 128
EPS = 1e-5

bfdt = ml_dtypes.bfloat16

ENGINES = ("tensor", "vector", "scalar", "gpsimd", "sync")



def tile_lhsT(w):
    fo, fi = w.shape
    assert fo % 128 == 0 and fi % 128 == 0, (fo, fi)
    nt, nc = fo // 128, fi // 128
    return np.ascontiguousarray(
        w.reshape(nt, 128, nc, 128).transpose(0, 3, 2, 1))


def tile_A(a):
    r, fi = a.shape
    nc = fi // 128
    return np.ascontiguousarray(a.T.reshape(nc, 128, r).transpose(1, 0, 2))


def tile_B(b):
    fo, r = b.shape
    nt = fo // 128
    return np.ascontiguousarray(b.T.reshape(r, nt, 128))



class Prog:

    def __init__(self, nc):
        self.nc = nc
        self.ops = {e: [] for e in ENGINES}
        self.counts = {}
        self.handles = {}

    def sem(self, name):
        self.counts[name] = 0
        return name

    def cnt(self, name):
        return self.counts[name]

    def dma(self, eng, out, in_, sem, **kw):
        self.counts[sem] += 16
        h = self.handles

        def f(e, out=out, in_=in_, sem=sem, kw=kw):
            e.dma_start(out=out, in_=in_, **kw).then_inc(h[sem], 16)
        self.ops[eng].append(f)
        return self.counts[sem]

    def emit(self, eng, fn, sem=None, n=1):
        if sem is None:
            self.ops[eng].append(fn)
            return None
        self.counts[sem] += n
        h = self.handles

        def f(e, fn=fn, sem=sem, n=n):
            fn(e).then_inc(h[sem], n)
        self.ops[eng].append(f)
        return self.counts[sem]

    def wait(self, eng, sem, val=None):
        v = self.counts[sem] if val is None else val
        if v <= 0:
            return
        h = self.handles

        def f(e, sem=sem, v=v):
            e.wait_ge(h[sem], v)
        self.ops[eng].append(f)

    def build_block(self, block):
        for e in ENGINES:
            ops = self.ops[e]
            if not ops:
                continue

            def body(eng, ops=ops):
                for fn in ops:
                    fn(eng)
            getattr(block, e)(body)


def open_sems(P, stack, names):
    for nm in names:
        P.sem(nm)
        P.handles[nm] = stack.enter_context(P.nc.semaphore(nm))



def emit_ln(P, tag, x_d, pre, sb, ps, y_d=None):
    nc = P.nc
    sx, sq_, smm, sst, srep = (P.sem(tag + c) for c in "xqmsr")
    xb, sq, ab, row = sb["xb"], sb["sq"], sb["ab"], sb["row"]
    nst = 0
    for i in range(ND):
        if i > 0:
            P.wait("sync", smm, 4 * i)
        P.dma("sync", xb[:, 0:TC], x_d[i * 128:(i + 1) * 128, :], sx)
        if y_d is not None:
            P.dma("sync", sq[:, 0:TC], y_d[i * 128:(i + 1) * 128, :], sx)
            P.wait("vector", sx, P.cnt(sx))
            P.emit("vector", lambda e: e.tensor_tensor(
                xb[:, 0:TC], xb[:, 0:TC], sq[:, 0:TC], op=ALU.add), sem=sq_)
            P.wait("scalar", sq_, P.cnt(sq_))
        else:
            P.wait("scalar", sx, P.cnt(sx))
        P.emit("scalar", lambda e: e.activation(
            out=sq[:, 0:TC], in_=xb[:, 0:TC], func=AF.Square), sem=sq_)
        if y_d is not None:
            P.wait("tensor", sq_, P.cnt(sq_) - 1)
        else:
            P.wait("tensor", sx, P.cnt(sx))
        for hh in range(2):
            t = slice(hh * TH, hh * TH + TH)
            P.emit("tensor", lambda e, t=t, i=i: e.matmul(
                ps["s1"][0:1, t], pre["ones_col"][:, 0:1], xb[:, 0:TC][:, t],
                start=(i == 0), stop=(i == ND - 1)), sem=smm)
        P.wait("tensor", sq_, P.cnt(sq_))
        for hh in range(2):
            t = slice(hh * TH, hh * TH + TH)
            P.emit("tensor", lambda e, t=t, i=i: e.matmul(
                ps["s2"][0:1, t], pre["ones_col"][:, 0:1], sq[:, 0:TC][:, t],
                start=(i == 0), stop=(i == ND - 1)), sem=smm)
    r0 = row[0:1, 0:TC]; r1 = row[0:1, TC:2 * TC]
    r2 = row[0:1, 2 * TC:3 * TC]; r3 = row[0:1, 3 * TC:4 * TC]
    m, ex2, msq, var, stdv, rstd, brow = r0, r1, r2, r1, r2, r3, r2
    P.wait("vector", smm, P.cnt(smm))
    P.emit("vector", lambda e: e.tensor_scalar(m, ps["s1"][0:1, :], 1.0 / D, None, op0=ALU.mult), sem=sst)
    P.emit("vector", lambda e: e.tensor_scalar(ex2, ps["s2"][0:1, :], 1.0 / D, None, op0=ALU.mult), sem=sst)
    P.wait("scalar", sst, P.cnt(sst))
    P.emit("scalar", lambda e: e.activation(out=msq, in_=m, func=AF.Square), sem=sst)
    P.wait("vector", sst, P.cnt(sst))
    P.emit("vector", lambda e: e.tensor_tensor(var, ex2, msq, op=ALU.subtract), sem=sst)
    P.wait("scalar", sst, P.cnt(sst))
    P.emit("scalar", lambda e: e.activation(out=stdv, in_=var, func=AF.Ln, bias=pre["eps"][0:1, 0:1]), sem=sst)
    P.emit("scalar", lambda e: e.activation(out=rstd, in_=stdv, func=AF.Exp, scale=pre["neghalf"][0:1, 0:1]), sem=sst)
    P.wait("vector", sst, P.cnt(sst))
    P.emit("vector", lambda e: e.tensor_tensor(brow, m, rstd, op=ALU.mult), sem=sst)
    P.wait("tensor", sst, P.cnt(sst))
    for k, rsrc in ((0, rstd), (1, brow)):
        if k == 1:
            P.wait("tensor", srep, 1)
        for hh in range(2):
            t = slice(hh * TH, hh * TH + TH)
            P.emit("tensor", lambda e, t=t, rsrc=rsrc: e.matmul(
                ps["rep"][:, t], pre["ones_row"][0:1, :], rsrc[:, t],
                start=True, stop=True), sem=smm)
        P.wait("vector", smm, P.cnt(smm))
        P.emit("vector", lambda e, k=k: e.tensor_copy(ab[:, k * TC:(k + 1) * TC], ps["rep"][:, :]), sem=srep)
    return srep


def emit_xln_half(P, tag, x_d, sb, half, xln_bf, y_d=None, sync_gate=None, vec_gate=None):
    sx, sl = P.sem(tag + "X"), P.sem(tag + "L")
    xb, sq, ab = sb["xb"], sb["sq"], sb["ab"]
    t = slice(half * TH, half * TH + TH)
    if sync_gate is not None:
        P.wait("sync", sync_gate, P.cnt(sync_gate))
    if vec_gate is not None:
        P.wait("vector", vec_gate, P.cnt(vec_gate))
    for i in range(ND):
        if i > 0:
            P.wait("sync", sl, P.cnt(sl))
        P.dma("sync", xb[:, 0:TH], x_d[i * 128:(i + 1) * 128, t], sx)
        if y_d is not None:
            P.dma("sync", xb[:, TH:TC], y_d[i * 128:(i + 1) * 128, t], sx)
        P.wait("vector", sx, P.cnt(sx))
        if y_d is not None:
            P.emit("vector", lambda e: e.tensor_tensor(
                xb[:, 0:TH], xb[:, 0:TH], xb[:, TH:TC], op=ALU.add), sem=sl)
            P.wait("vector", sl, P.cnt(sl))
        o = slice(i * TH, (i + 1) * TH)
        P.emit("vector", lambda e, o=o: e.tensor_tensor(
            sq[:, 0:TH], xb[:, 0:TH], ab[:, 0:TC][:, t], op=ALU.mult), sem=sl)
        P.wait("vector", sl, P.cnt(sl))
        P.emit("vector", lambda e, o=o: e.tensor_tensor(
            xln_bf[:, o], sq[:, 0:TH], ab[:, TC:2 * TC][:, t], op=ALU.subtract), sem=sl)
    return sl


def emit_ffn0(P, tag, x_d, xout_d, w, pre, sb):
    nc = P.nc
    with nc.psum_tensor(tag + "ps1", [1, TC], F32) as ps1, \
         nc.psum_tensor(tag + "ps2", [1, TC], F32) as ps2, \
         nc.psum_tensor(tag + "psr", [128, TC], F32) as psr:
        lnrep = emit_ln(P, tag + "N", x_d, pre, sb, {"s1": ps1, "s2": ps2, "rep": psr})

    sw0 = P.sem(tag + "w0")
    P.dma("gpsimd", sb["a1t"][:, :], w["a1t"].rearrange("p c r -> p (c r)"), sw0)
    P.dma("gpsimd", sb["a3t"][:, :], w["a3t"].rearrange("p c r -> p (c r)"), sw0)
    P.dma("gpsimd", sb["a2t"][:, :], w["a2t"].rearrange("p c r -> p (c r)"), sw0)
    P.dma("gpsimd", sb["b2t"][:, :], w["b2t"].rearrange("r t j -> r (t j)"), sw0)

    prev_sh3 = None
    for half in range(2):
        h = str(half)
        xln = sb["xln"]
        sln = emit_xln_half(P, tag + "H" + h, x_d, sb, half, xln, sync_gate=lnrep)
        if sb.get("dbg_early") is not None and half == 1:
            sde = P.sem(tag + "de")
            P.wait("sync", sln, P.cnt(sln))
            P.dma("sync", sb["dbg_early"], xln[:, :], sde)
            P.wait("sync", sde, P.cnt(sde))
        sw = (P.sem(tag + h + "wA"), P.sem(tag + h + "wB"))
        swb = (P.sem(tag + h + "bA"), P.sem(tag + h + "bB"))
        swq = (P.sem(tag + h + "qA"), P.sem(tag + h + "qB"))
        sh1 = P.sem(tag + h + "h1")
        sh3 = P.sem(tag + h + "h3")
        ssi = P.sem(tag + h + "si")
        sev = P.sem(tag + h + "ev")
        sw2 = P.sem(tag + h + "w2")
        sad = P.sem(tag + h + "ad")
        sxr = P.sem(tag + h + "xr")
        sxe = P.sem(tag + h + "xe")
        sxc = P.sem(tag + h + "xc")
        sfi = P.sem(tag + h + "fi")
        sxo = (P.sem(tag + h + "oA"), P.sem(tag + h + "oB"))

        with nc.psum_tensor(tag + h + "pA1", [128, TH], F32) as pA1, \
             nc.psum_tensor(tag + h + "pB1", [128, TH], F32) as pB1, \
             nc.psum_tensor(tag + h + "pA3", [128, TH], F32) as pA3, \
             nc.psum_tensor(tag + h + "pB3", [128, TH], F32) as pB3, \
             nc.psum_tensor(tag + h + "pxr", [16, TH], F32) as pxr, \
             nc.psum_tensor(tag + h + "pW2a", [128, TH], F32) as pW2a, \
             nc.psum_tensor(tag + h + "pW2b", [128, TH], F32) as pW2b:
            ph1 = (pA1, pB1)
            ph3 = (pA3, pB3)
            pw2 = (pW2a, pW2b)
            xr1b = sb["xrb"][0:16, 0:TH]
            xr3b = sb["xrb"][0:16, TH:2 * TH]
            xr2b = sb["xrb"][0:16, 2 * TH:3 * TH]

            P.wait("tensor", sln, P.cnt(sln))
            P.wait("tensor", sw0, P.cnt(sw0))
            for k, (at, dst) in enumerate(((sb["a1t"], xr1b), (sb["a3t"], xr3b))):
                if k == 1:
                    P.wait("tensor", sxe, 1)
                for c in range(ND):
                    P.emit("tensor", lambda e, at=at, c=c: e.matmul(
                        pxr[:, :], at[:, c * R:(c + 1) * R],
                        xln[:, c * TH:(c + 1) * TH],
                        start=(c == 0), stop=(c == ND - 1)), sem=sxr)
                P.wait("vector", sxr, P.cnt(sxr))
                P.emit("vector", lambda e, dst=dst: e.tensor_scalar(
                    dst, pxr[:, :], 2.0, None, op0=ALU.mult), sem=sxe)

            for b in range(NBLK):
                f0, f1 = b * FBLK, min((b + 1) * FBLK, NF)
                for f in range(f0, f1):
                    fp = f % 2
                    wsl = slice(fp * ND * 128, (fp + 1) * ND * 128)
                    bsl = slice(fp * 128, (fp + 1) * 128)
                    if f >= 2:
                        P.wait("gpsimd", sh3, f - 1)
                    elif prev_sh3 is not None:
                        P.wait("gpsimd", prev_sh3, P.cnt(prev_sh3))
                    P.dma("gpsimd", sb["w1s"][:, wsl],
                          w["w1t"][f].rearrange("p c j -> p (c j)"), sw[fp])
                    P.dma("gpsimd", sb["w3s"][:, wsl],
                          w["w3t"][f].rearrange("p c j -> p (c j)"), sw[fp])
                    P.dma("gpsimd", sb["b1s"][:, bsl],
                          w["b1t"][:, f, :], swb[fp])
                    P.dma("gpsimd", sb["b3s"][:, bsl],
                          w["b3t"][:, f, :], swb[fp])
                for f in range(f0, f1):
                    fp = f % 2
                    wsl = slice(fp * ND * 128, (fp + 1) * ND * 128)
                    bsl = slice(fp * 128, (fp + 1) * 128)
                    j = f - f0
                    if f > 0:
                        fprev = f - 1
                        jp = fprev - (fprev // FBLK) * FBLK
                        P.wait("tensor", sev, fprev + 1)
                        P.emit("tensor", lambda e, fprev=fprev, jp=jp: e.matmul(
                            pxr[:, :], sb["a2t"][:, fprev * R:(fprev + 1) * R],
                            sb["g"][:, jp * TH:(jp + 1) * TH],
                            start=(fprev == 0), stop=False), sem=sxr)
                    P.wait("tensor", sw[fp], 32 * (f // 2 + 1))
                    P.wait("tensor", swb[fp], 32 * (f // 2 + 1))
                    if f == 0:
                        P.wait("tensor", sxe, 2)
                    if f >= 2:
                        P.wait("tensor", ssi, f - 1)
                    for c in range(ND):
                        P.emit("tensor", lambda e, fp=fp, wsl=wsl, c=c: e.matmul(
                            ph1[fp][:, :], sb["w1s"][:, wsl][:, c * 128:(c + 1) * 128],
                            xln[:, c * TH:(c + 1) * TH], start=(c == 0), stop=False))
                    P.emit("tensor", lambda e, fp=fp, bsl=bsl: e.matmul(
                        ph1[fp][:, :], sb["b1s"][:, bsl],
                        xr1b, start=False, stop=True), sem=sh1)
                    if f >= 2:
                        P.wait("tensor", sev, f - 1)
                    for c in range(ND):
                        P.emit("tensor", lambda e, fp=fp, wsl=wsl, c=c: e.matmul(
                            ph3[fp][:, :], sb["w3s"][:, wsl][:, c * 128:(c + 1) * 128],
                            xln[:, c * TH:(c + 1) * TH], start=(c == 0), stop=False))
                    P.emit("tensor", lambda e, fp=fp, bsl=bsl: e.matmul(
                        ph3[fp][:, :], sb["b3s"][:, bsl],
                        xr3b, start=False, stop=True), sem=sh3)
                    P.wait("scalar", sh1, f + 1)
                    P.emit("scalar", lambda e, fp=fp: e.activation(
                        out=sb["s1"][:, fp * TH:(fp + 1) * TH], in_=ph1[fp][:, :],
                        func=AF.Silu), sem=ssi)
                    P.wait("vector", ssi, f + 1)
                    P.wait("vector", sh3, f + 1)
                    if b > 0 and f == f0:
                        P.wait("vector", sw2, ND * b)
                    P.emit("vector", lambda e, fp=fp, j=j: e.tensor_tensor(
                        sb["g"][:, j * TH:(j + 1) * TH],
                        sb["s1"][:, fp * TH:(fp + 1) * TH], ph3[fp][:, :],
                        op=ALU.mult), sem=sev)

                last = b == NBLK - 1
                if last:
                    P.wait("tensor", sev, NF)
                    jl = NF - 1 - f0
                    P.emit("tensor", lambda e, jl=jl: e.matmul(
                        pxr[:, :], sb["a2t"][:, (NF - 1) * R:NF * R],
                        sb["g"][:, jl * TH:(jl + 1) * TH],
                        start=False, stop=True), sem=sxr)
                    P.wait("vector", sxr, P.cnt(sxr))
                    P.emit("vector", lambda e: e.tensor_scalar(
                        xr2b, pxr[:, :], 2.0, None, op0=ALU.mult), sem=sxe)
                for ot in range(ND):
                    k = b * ND + ot
                    op = ot % 2
                    o2sl = slice(op * FBLK * 128, op * FBLK * 128 + (f1 - f0) * 128)
                    if k >= 2:
                        P.wait("gpsimd", sw2, k - 1)
                    P.dma("gpsimd", sb["w2s"][:, o2sl],
                          w["w2t"][ot, :, f0:f1, :].rearrange("p c j -> p (c j)"), swq[k % 2])
                    P.wait("tensor", swq[k % 2], 16 * (k // 2 + 1))
                    if k >= 2:
                        P.wait("tensor", sad, k - 1)
                    if ot == 0:
                        P.wait("tensor", sev, f1)
                        if last:
                            P.wait("tensor", sxe, 3)
                    nsub = f1 - f0
                    for j in range(nsub):
                        fin = (j == nsub - 1) and not last
                        P.emit("tensor", lambda e, op=op, o2sl=o2sl, j=j, fin=fin: e.matmul(
                            pw2[op][:, :],
                            sb["w2s"][:, o2sl][:, j * 128:(j + 1) * 128],
                            sb["g"][:, j * TH:(j + 1) * TH],
                            start=(j == 0), stop=fin),
                            sem=(sw2 if fin else None))
                    if last:
                        P.emit("tensor", lambda e, op=op, ot=ot: e.matmul(
                            pw2[op][:, :], sb["b2t"][:, ot * 128:(ot + 1) * 128],
                            xr2b, start=False, stop=True), sem=sw2)
                    P.wait("vector", sw2, P.cnt(sw2))
                    P.wait("vector", sad, P.cnt(sad))
                    osl = slice(ot * TH, (ot + 1) * TH)
                    if b == 0:
                        P.emit("vector", lambda e, op=op, osl=osl: e.tensor_copy(
                            sb["acc"][:, osl], pw2[op][:, :]), sem=sad)
                    else:
                        P.emit("vector", lambda e, op=op, osl=osl: e.tensor_tensor(
                            sb["acc"][:, osl], sb["acc"][:, osl], pw2[op][:, :],
                            op=ALU.add), sem=sad)

            t = slice(half * TH, half * TH + TH)
            for ot in range(ND):
                cp = ot % 2
                csl = slice(cp * TH, (cp + 1) * TH)
                if ot == 0:
                    P.wait("sync", sln, P.cnt(sln))
                if ot >= 2:
                    P.wait("sync", sfi, ot - 1)
                P.dma("sync", sb["xc"][:, csl], x_d[ot * 128:(ot + 1) * 128, t], sxc)
                P.wait("vector", sxc, P.cnt(sxc))
                P.wait("vector", sad, P.cnt(sad))
                if ot >= 2:
                    P.wait("vector", sxo[cp], 16 * (ot // 2))
                osl = slice(ot * TH, (ot + 1) * TH)
                P.emit("vector", lambda e, cp=cp, csl=csl, osl=osl: e.scalar_tensor_tensor(
                    sb["xo"][:, csl], sb["acc"][:, osl], 0.5, sb["xc"][:, csl],
                    op0=ALU.mult, op1=ALU.add), sem=sfi)
                P.wait("sync", sfi, P.cnt(sfi))
                P.dma("sync", xout_d[ot * 128:(ot + 1) * 128, t], sb["xo"][:, csl], sxo[cp])
        prev_sh3 = sh3
        last_sxo = sxo
    return last_sxo


def alloc_ffn0_sbuf(nc, stack, tag):
    S = {}
    def sb(name, shape, dt):
        S[name] = stack.enter_context(nc.sbuf_tensor(tag + name, shape, dt))
        return S[name]
    sb("xln", [128, ND * TH], BF16)
    sb("acc", [128, ND * TH], F32)
    sb("g", [128, FBLK * TH], BF16)
    sb("w1s", [128, 2 * ND * 128], BF16)
    sb("w3s", [128, 2 * ND * 128], BF16)
    sb("w2s", [128, 2 * FBLK * 128], BF16)
    sb("b1s", [16, 2 * 128], BF16)
    sb("b3s", [16, 2 * 128], BF16)
    sb("a1t", [128, ND * R], BF16)
    sb("a3t", [128, ND * R], BF16)
    sb("a2t", [128, NF * R], BF16)
    sb("b2t", [16, ND * 128], BF16)
    sb("xrb", [16, 3 * TH], BF16)
    sb("s1", [128, 2 * TH], F32)
    sb("xc", [128, 2 * TH], F32)
    sb("xo", [128, 2 * TH], F32)
    sb("ab", [128, 2 * TC], F32)
    sb("row", [1, 4 * TC], F32)
    S["xb"] = S["xc"]
    S["sq"] = S["xo"]
    return S


def make_pre(nc, stack, P):
    pre = {}
    pre["ones_col"] = stack.enter_context(nc.sbuf_tensor("ones_col", [128, 1], F32))
    pre["ones_row"] = stack.enter_context(nc.sbuf_tensor("ones_row", [1, 128], F32))
    pre["eps"] = stack.enter_context(nc.sbuf_tensor("epsc", [1, 1], F32))
    pre["neghalf"] = stack.enter_context(nc.sbuf_tensor("nhc", [1, 1], F32))
    pre["halfc"] = stack.enter_context(nc.sbuf_tensor("hfc", [128, 1], F32))
    pre["negc"] = stack.enter_context(nc.sbuf_tensor("ngc", [128, 1], F32))
    s = P.sem("pre")
    P.emit("vector", lambda e: e.memset(pre["ones_col"][:, :], 1.0), sem=s)
    P.emit("vector", lambda e: e.memset(pre["ones_row"][:, :], 1.0), sem=s)
    P.emit("vector", lambda e: e.memset(pre["eps"][:, :], EPS), sem=s)
    P.emit("vector", lambda e: e.memset(pre["neghalf"][:, :], -0.5), sem=s)
    P.emit("vector", lambda e: e.memset(pre["halfc"][:, :], 0.5), sem=s)
    P.emit("vector", lambda e: e.memset(pre["negc"][:, :], -1.0), sem=s)
    P.wait("tensor", s, 6)
    P.wait("scalar", s, 6)
    return pre


def emit_proj(P, tag, x_d, w_t, bias_sb, nft, out_bf_d, pre, sb, scale=None, gate=None):
    nc = P.nc
    with nc.psum_tensor(tag + "pa", [128, TH], F32) as pa, \
         nc.psum_tensor(tag + "pb", [128, TH], F32) as pb:
        pp = (pa, pb)
        prev_smm = None
        prev_sev = None
        prev_sso = None
        for half in range(2):
            h = str(half)
            xln = sb["xln"]
            sln = emit_xln_half(P, tag + "L" + h, x_d, sb, half, xln,
                                sync_gate=gate if half == 0 else prev_smm,
                                vec_gate=prev_smm)
            sw = (P.sem(tag + h + "wA"), P.sem(tag + h + "wB"))
            if gate is not None and half == 0:
                P.wait("gpsimd", gate, P.cnt(gate))
            smm = P.sem(tag + h + "m")
            sev = P.sem(tag + h + "e")
            sso = (P.sem(tag + h + "oA"), P.sem(tag + h + "oB"))
            P.wait("tensor", sln, P.cnt(sln))
            if prev_sev is not None:
                P.wait("tensor", prev_sev, P.cnt(prev_sev))
            t = slice(half * TH, half * TH + TH)
            for ft in range(nft):
                fp = ft % 2
                wsl = slice(fp * ND * 128, (fp + 1) * ND * 128)
                if ft >= 2:
                    P.wait("gpsimd", smm, ft - 1)
                elif prev_smm is not None:
                    P.wait("gpsimd", prev_smm, P.cnt(prev_smm))
                P.dma("gpsimd", sb["w1s"][:, wsl],
                      w_t[ft].rearrange("p c j -> p (c j)"), sw[fp])
                P.wait("tensor", sw[fp], 16 * (ft // 2 + 1))
                if ft >= 2:
                    P.wait("tensor", sev, ft - 1)
                for c in range(ND):
                    fin = c == ND - 1
                    P.emit("tensor", lambda e, fp=fp, wsl=wsl, c=c, fin=fin: e.matmul(
                        pp[fp][:, :], sb["w1s"][:, wsl][:, c * 128:(c + 1) * 128],
                        xln[:, c * TH:(c + 1) * TH], start=(c == 0), stop=fin),
                        sem=(smm if fin else None))
                P.wait("scalar", smm, P.cnt(smm))
                if ft < 2 and prev_sso is not None:
                    for s_ in prev_sso:
                        P.wait("scalar", s_, P.cnt(s_))
                if ft >= 2:
                    P.wait("scalar", sso[fp], 16 * (ft // 2))
                kw = {}
                P.emit("scalar", lambda e, fp=fp, ft=ft: e.activation(
                    out=sb["s1b"][:, fp * TH:(fp + 1) * TH], in_=pp[fp][:, :],
                    func=AF.Identity,
                    **({"bias": bias_sb[:, ft:ft + 1]} if bias_sb is not None else {}),
                    **({"scale": scale} if scale is not None else {})), sem=sev)
                P.wait("sync", sev, P.cnt(sev))
                P.dma("sync", out_bf_d[ft * 128:(ft + 1) * 128, t],
                      sb["s1b"][:, fp * TH:(fp + 1) * TH], sso[fp])
            for s_ in sso:
                P.wait("sync", s_, P.cnt(s_))
            prev_smm = smm
            prev_sev = sev
            prev_sso = sso
    return None



def build_l1(skip_qkv=False, debug_dump=False):
    nc = bass.Bass("TRN2", target_bir_lowering=False, debug=False)
    dr = {}
    def din(name, shape, dt=F32):
        dr[name] = nc.dram_tensor(name, shape, dt, kind="ExternalInput").ap()
        return dr[name]
    x = din("x", [D, TC])
    w = {
        "w1t": din("w1t", [NF, 128, ND, 128]),
        "w3t": din("w3t", [NF, 128, ND, 128]),
        "w2t": din("w2t", [ND, 128, NF, 128]),
        "a1t": din("a1t", [128, ND, R]),
        "a3t": din("a3t", [128, ND, R]),
        "a2t": din("a2t", [128, NF, R]),
        "b1t": din("b1t", [R, NF, 128]),
        "b3t": din("b3t", [R, NF, 128]),
        "b2t": din("b2t", [R, ND, 128]),
    }
    if not skip_qkv:
        wqkvt = din("wqkvt", [12, 128, ND, 128])
        bqkv = din("bqkv", [128, 12])
    xout = nc.dram_tensor("xout", [D, TC], F32, kind="ExternalOutput").ap()
    if not skip_qkv:
        qkv = nc.dram_tensor("qkv", [12 * 128, TC], BF16, kind="ExternalOutput").ap()

    with ExitStack() as st:
        P = Prog(nc)
        sb = alloc_ffn0_sbuf(nc, st, "m")
        sb["s1b"] = st.enter_context(nc.sbuf_tensor("s1b", [128, 2 * TH], BF16))
        if not skip_qkv:
            sbq = st.enter_context(nc.sbuf_tensor("bqkv_sb", [128, 12], F32))
        pre = make_pre(nc, st, P)
        if debug_dump:
            sb["dbg_early"] = nc.dram_tensor("dbg_early", [128, ND * TH], BF16, kind="ExternalOutput").ap()
        if not skip_qkv:
            sc0 = P.sem("c0")
            P.dma("gpsimd", sbq[:, :], bqkv, sc0)
            P.wait("scalar", sc0, 16)
        sxo = emit_ffn0(P, "F", x, xout, w, pre, sb)
        if debug_dump:
            sdbg = P.sem("dbg")
            for s_ in sxo:
                P.wait("sync", s_, P.cnt(s_))
            for nm, shp in (("ab", None), ("xln", None), ("acc", None), ("xrb", None), ("row", None)):
                hnd = sb[nm]
                dout = nc.dram_tensor("dbg_" + nm, list(hnd.shape), hnd.dtype, kind="ExternalOutput").ap()
                P.dma("sync", dout, hnd[:, :], sdbg)
            P.wait("sync", sdbg, P.cnt(sdbg))
        if not skip_qkv:
            with nc.psum_tensor("qs1", [1, TC], F32) as ps1, \
                 nc.psum_tensor("qs2", [1, TC], F32) as ps2, \
                 nc.psum_tensor("qsr", [128, TC], F32) as psr:
                for s_ in sxo:
                    P.wait("sync", s_, P.cnt(s_))
                qrep = emit_ln(P, "Q", xout, pre, sb, {"s1": ps1, "s2": ps2, "rep": psr})
            emit_proj(P, "QP", xout, wqkvt, sbq, 12, qkv, pre, sb, gate=qrep)
        names = list(P.counts.keys())
        open_needed = [n for n in names if n not in P.handles]
        for nm in open_needed:
            P.handles[nm] = st.enter_context(nc.semaphore(nm))
        with nc.Block() as block:
            P.build_block(block)
    return nc


SQRT_HD = float(np.sqrt(512.0))


def emit_attn(P, tag, x_d, qT_d, kT_d, vT_d, wot_d, wob_sb, idb, idf, x2_d, pre, sb,
              nrow_d=None, nms_d=None):
    nc = P.nc
    NKT = T // 128
    NKC = T // TH
    skv = (P.sem(tag + "kvA"), P.sem(tag + "kvB"))
    sqr = P.sem(tag + "qr")
    svt = P.sem(tag + "vt")
    svm = P.sem(tag + "vm")
    ssp = P.sem(tag + "sp")
    srd = P.sem(tag + "rd")
    snq = P.sem(tag + "nq")
    stm = P.sem(tag + "tm")
    sne = P.sem(tag + "ne")
    sst = P.sem(tag + "st")
    sex = P.sem(tag + "ex")
    spk = P.sem(tag + "pk")
    slc = P.sem(tag + "lc")
    snr = P.sem(tag + "nr")
    snb = P.sem(tag + "nb")
    sam = P.sem(tag + "am")
    sow = (P.sem(tag + "owA"), P.sem(tag + "owB"))
    som = P.sem(tag + "om")
    sxc = P.sem(tag + "xc")
    sfi = P.sem(tag + "fi")
    sxo = (P.sem(tag + "xoA"), P.sem(tag + "xoB"))

    kaug, vtmp, vab = sb["kaug"], sb["vtmp"], sb["vab"]
    qaug, ptb, attnall = sb["qaug"], sb["ptb"], sb["attnall"]
    cm, nmc, nms, lbuf, nrep = sb["cm"], sb["nmc"], sb["nms"], sb["lbuf"], sb["nrep"]

    sone = P.sem(tag + "one")
    P.emit("vector", lambda e: e.memset(kaug[64:65, :], 1.0), sem=sone)
    if sb.get("_nheads", NHEAD) < NHEAD:
        P.emit("vector", lambda e: e.memset(sb["attnall"][:, :], 0.0), sem=sone)

    with nc.psum_tensor(tag + "sta", [128, TC], F32) as sta, \
         nc.psum_tensor(tag + "stb", [128, TC], F32) as stb, \
         nc.psum_tensor(tag + "ppv", [65, TC], F32) as ppv, \
         nc.psum_tensor(tag + "pvt", [128, 4 * 128], BF16) as pvt, \
         nc.psum_tensor(tag + "pnm", [8, 128], F32) as pnm:
        pst = (sta, stb)
        nheads = sb.get("_nheads", NHEAD)
        for hd in range(nheads):
            if hd > 0:
                P.wait("sync", sst, NKT * hd)
            hp = hd % 2
            P.dma("sync", kaug[0:64, :], kT_d[64 * hd:64 * hd + 64, :], skv[hp])
            P.dma("sync", vtmp[0:64, :], vT_d[64 * hd:64 * hd + 64, :], skv[hp])
            P.dma("sync", qaug[0:64, :], qT_d[64 * hd:64 * hd + 64, :], skv[hp])
            P.wait("vector", spk, NKT * hd)
            P.emit("vector", lambda e: e.memset(vab[:, :], 1.0), sem=svt)
            P.wait("tensor", skv[hp], 48 * (hd // 2 + 1))
            for bt in range(8):
                P.wait("tensor", svt, 9 * hd + bt + 1)
                for sub in range(8):
                    kt = bt * 8 + sub
                    P.emit("tensor", lambda e, kt=kt, sub=sub: e.transpose(
                        pvt[:, sub * 64:(sub + 1) * 64],
                        vtmp[0:64, kt * 128:(kt + 1) * 128],
                        idb[0:64, 0:64]), sem=(svm if sub == 7 else None))
                P.wait("vector", svm, 8 * hd + bt + 1)
                P.wait("vector", svt, 9 * hd + 1)
                off = bt * 8 * 65
                P.emit("vector", lambda e, off=off: e.tensor_copy(
                    vab[:, off:off + 8 * 65].rearrange("p (b e) -> p b e", e=65)[:, :, 0:64],
                    pvt[:, 0:512].rearrange("p (b e) -> p b e", e=64)), sem=svt)
            P.wait("tensor", sone, 1)
            for qt in range(8):
                for kc in range(NKC):
                    ci = (8 * hd + qt) * NKC + kc
                    pp = pst[kc % 2][:, 0:TH]
                    if kc == 0:
                        P.wait("vector", snq, P.cnt(snq))
                    if ci >= 2:
                        P.wait("tensor", srd, ci - 1)
                    P.emit("tensor", lambda e, pp=pp, qt=qt, kc=kc: e.matmul(
                        pp, qaug[0:64, qt * 128:(qt + 1) * 128],
                        kaug[0:64, kc * TH:(kc + 1) * TH],
                        start=True, stop=True), sem=ssp)
                    P.wait("vector", ssp, P.cnt(ssp))
                    P.emit("vector", lambda e, pp=pp, kc=kc: e.tensor_reduce(
                        cm[:, kc:kc + 1], pp, axis=AX.X, op=ALU.max), sem=srd)
                P.wait("vector", srd, P.cnt(srd))
                P.emit("vector", lambda e, qt=qt: e.tensor_reduce(
                    nmc[:, qt:qt + 1], cm[:, :], axis=AX.X, op=ALU.max,
                    negate=True), sem=snq)
            P.wait("tensor", snq, 8 * (hd + 1))
            P.wait("tensor", sne, hd)
            P.emit("tensor", lambda e: e.transpose(
                pnm[:, :], nmc[:, 0:8], idf[:, :]), sem=stm)
            P.wait("vector", stm, hd + 1)
            P.wait("vector", sqr, P.cnt(sqr))
            P.emit("vector", lambda e: e.tensor_copy(nms[0:8, :], pnm[:, :]), sem=sne)
            P.wait("gpsimd", sne, hd + 1)
            P.dma("gpsimd", nms_d, nms[0:8, :], sqr)
            P.wait("gpsimd", sqr, P.cnt(sqr))
            P.dma("gpsimd", qaug[64:65, :],
                  nms_d.rearrange("b t -> (b t)")[None, :], sqr)
            P.wait("tensor", sqr, 32 * (hd + 1))
            P.wait("tensor", sam, hd)
            P.wait("tensor", slc, hd)
            for kt in range(NKT):
                kk = NKT * hd + kt
                pp = pst[kt % 2]
                if kk >= 2:
                    P.wait("tensor", sex, kk - 1)
                for sub in range(2):
                    P.emit("tensor", lambda e, pp=pp, kt=kt, sub=sub: e.matmul(
                        pp[:, sub * TH:(sub + 1) * TH],
                        kaug[:, kt * 128:(kt + 1) * 128],
                        qaug[:, sub * TH:(sub + 1) * TH],
                        start=True, stop=True), sem=(sst if sub == 1 else None))
                P.wait("scalar", sst, P.cnt(sst))
                if kk >= 2:
                    P.wait("scalar", spk, kk - 1)
                P.emit("scalar", lambda e, pp=pp, kt=kt: e.activation(
                    out=ptb[:, (kt % 2) * TC:(kt % 2) * TC + TC], in_=pp[:, :],
                    func=AF.Exp), sem=sex)
                P.wait("tensor", sex, P.cnt(sex))
                for sub in range(2):
                    P.emit("tensor", lambda e, kt=kt, sub=sub, hd=hd: e.matmul(
                        ppv[:, sub * TH:(sub + 1) * TH],
                        vab[:, kt * 65:(kt + 1) * 65],
                        ptb[:, (kt % 2) * TC + sub * TH:(kt % 2) * TC + (sub + 1) * TH],
                        start=(kt == 0), stop=(kt == NKT - 1)),
                        sem=(spk if sub == 1 else None))
            P.wait("scalar", spk, NKT * (hd + 1))
            P.emit("scalar", lambda e: e.copy(lbuf[64:65, 0:TC], ppv[64:65, :]), sem=slc)
            P.wait("vector", slc, hd + 1)
            P.wait("vector", snb, P.cnt(snb))
            P.emit("vector", lambda e: e.tensor_scalar(
                lbuf[64:65, TC:2 * TC], lbuf[64:65, 0:TC], SQRT_HD, None,
                op0=ALU.mult), sem=snr)
            P.wait("vector", snr, P.cnt(snr))
            P.emit("vector", lambda e: e.reciprocal(
                lbuf[64:65, 0:TC], lbuf[64:65, TC:2 * TC]), sem=snr)
            P.wait("gpsimd", snr, 2 * (hd + 1))
            P.dma("gpsimd", nrow_d, lbuf[64:65, 0:TC], snb)
            P.wait("gpsimd", snb, P.cnt(snb))
            P.dma("gpsimd", nrep[0:64, :], nrow_d.broadcast_to([64, TC]), snb)
            P.wait("vector", snb, 32 * (hd + 1))
            at = attnall[:, (hd // 2) * TC:(hd // 2 + 1) * TC]
            rows = slice((hd % 2) * 64, (hd % 2) * 64 + 64)
            P.emit("vector", lambda e, at=at, rows=rows: e.tensor_tensor(
                at[rows, :], ppv[0:64, :], nrep[0:64, :], op=ALU.mult), sem=sam)

        P.wait("tensor", sam, nheads)
        for ot in range(ND):
            op = ot % 2
            osl = slice(op * 4 * 128, (op + 1) * 4 * 128)
            if ot >= 2:
                P.wait("gpsimd", som, 2 * (ot - 1))
            P.dma("gpsimd", sb["wos"][:, osl],
                  wot_d[ot].rearrange("p c j -> p (c j)"), sow[op])
            P.wait("tensor", sow[op], 16 * (ot // 2 + 1))
            if ot >= 2:
                P.wait("tensor", sfi, 2 * (ot - 1))
            for sub in range(2):
                pp = pst[op][:, sub * TH:(sub + 1) * TH]
                for c in range(4):
                    P.emit("tensor", lambda e, pp=pp, osl=osl, c=c, sub=sub: e.matmul(
                        pp, sb["wos"][:, osl][:, c * 128:(c + 1) * 128],
                        attnall[:, c * TC + sub * TH:c * TC + (sub + 1) * TH],
                        start=(c == 0), stop=(c == 3)), sem=(som if c == 3 else None))
            if ot >= 2:
                P.wait("sync", sfi, 2 * (ot - 1))
            P.dma("sync", sb["xc2"][:, op * TC:op * TC + TC],
                  x_d[ot * 128:(ot + 1) * 128, :], sxc)
            P.wait("vector", sxc, P.cnt(sxc))
            P.wait("vector", som, 2 * (ot + 1))
            if ot >= 2:
                P.wait("vector", sxo[op], 16 * (ot // 2))
            for sub in range(2):
                pp = pst[op][:, sub * TH:(sub + 1) * TH]
                tsl = slice(sub * TH, (sub + 1) * TH)
                P.emit("vector", lambda e, pp=pp, op=op, ot=ot, tsl=tsl: e.scalar_tensor_tensor(
                    sb["xo2"][:, op * TC:op * TC + TC][:, tsl], pp,
                    wob_sb[:, ot:ot + 1],
                    sb["xc2"][:, op * TC:op * TC + TC][:, tsl],
                    op0=ALU.add, op1=ALU.add), sem=sfi)
            P.wait("sync", sfi, P.cnt(sfi))
            P.dma("sync", x2_d[ot * 128:(ot + 1) * 128, :],
                  sb["xo2"][:, op * TC:op * TC + TC], sxo[op])
    return sxo


def alloc_attn_sbuf(nc, stack, tag):
    S = {}
    def sb(name, shape, dt):
        S[name] = stack.enter_context(nc.sbuf_tensor(tag + name, shape, dt))
        return S[name]
    sb("kaug", [65, T], BF16)
    sb("vtmp", [64, T], BF16)
    sb("vab", [128, 64 * 65], BF16)
    sb("qaug", [65, TC], BF16)
    sb("ptb", [128, 2 * TC], BF16)
    sb("attnall", [128, 4 * TC], BF16)
    sb("cm", [128, 16], F32)
    sb("nmc", [128, 8], F32)
    sb("nms", [8, 128], F32)
    sb("lbuf", [128, 2 * TC], F32)
    sb("nrep", [64, TC], F32)
    sb("wos", [128, 2 * 4 * 128], BF16)
    sb("xc2", [128, 2 * TC], F32)
    sb("xo2", [128, 2 * TC], F32)
    return S


def build_l2_attn_only(nheads=NHEAD, do_oproj=True):
    nc = bass.Bass("TRN2", target_bir_lowering=False, debug=False)
    x = nc.dram_tensor("x", [D, TC], F32, kind="ExternalInput").ap()
    qT = nc.dram_tensor("qT", [512, TC], BF16, kind="ExternalInput").ap()
    kT = nc.dram_tensor("kT", [512, T], BF16, kind="ExternalInput").ap()
    vT = nc.dram_tensor("vT", [512, T], BF16, kind="ExternalInput").ap()
    wot = nc.dram_tensor("wot", [ND, 128, 4, 128], F32, kind="ExternalInput").ap()
    wob = nc.dram_tensor("wob", [128, ND], F32, kind="ExternalInput").ap()
    idb = nc.dram_tensor("idb", [128, 128], BF16, kind="ExternalInput").ap()
    idf = nc.dram_tensor("idf", [128, 128], F32, kind="ExternalInput").ap()
    x2 = nc.dram_tensor("x2", [D, TC], F32, kind="ExternalOutput").ap()
    nrow_d = nc.dram_tensor("nrow_s", [1, TC], F32).ap()
    nms_d = nc.dram_tensor("nms_s", [8, 128], F32).ap()

    with ExitStack() as st:
        P = Prog(nc)
        sb = alloc_attn_sbuf(nc, st, "a")
        sb["_nheads"] = nheads
        idb_sb = st.enter_context(nc.sbuf_tensor("idb_sb", [128, 128], BF16))
        idf_sb = st.enter_context(nc.sbuf_tensor("idf_sb", [128, 128], F32))
        wob_sb = st.enter_context(nc.sbuf_tensor("wob_sb", [128, ND], F32))
        pre = make_pre(nc, st, P)
        sc0 = P.sem("c0")
        P.dma("gpsimd", idb_sb[:, :], idb, sc0)
        P.dma("gpsimd", idf_sb[:, :], idf, sc0)
        P.dma("gpsimd", wob_sb[:, :], wob, sc0)
        P.wait("tensor", sc0, P.cnt(sc0))
        P.wait("vector", sc0, P.cnt(sc0))
        sxo = emit_attn(P, "A", x, qT, kT, vT, wot, wob_sb, idb_sb, idf_sb,
                        x2, pre, sb, nrow_d=nrow_d, nms_d=nms_d)
        for s_ in sxo:
            P.wait("sync", s_, P.cnt(s_))
        for nm in P.counts:
            if nm not in P.handles:
                P.handles[nm] = st.enter_context(nc.semaphore(nm))
        with nc.Block() as block:
            P.build_block(block)
    return nc


def emit_gate_masks(P, tag, lsi_sb, pre, gsb):
    sg = P.sem(tag + "g")
    sgm = P.sem(tag + "gm")
    gt = gsb["gt"]
    c = gsb["cvec"]
    V = "vector"
    def ch():
        P.wait(V, sg, P.cnt(sg))
    P.emit(V, lambda e: e.tensor_scalar(c[0:1, :], lsi_sb[0:1, :], 0.0, None, op0=ALU.is_gt), sem=sg)
    ch()
    P.emit(V, lambda e: e.tensor_tensor(gt[0:1, 0:1], c[0:1, 0:1], c[0:1, 1:2], op=ALU.mult), sem=sg)
    ch()
    P.emit(V, lambda e: e.tensor_scalar(gt[0:1, 1:2], c[0:1, 1:2], -1.0, 2.0, op0=ALU.mult, op1=ALU.add), sem=sg)
    ch()
    P.emit(V, lambda e: e.tensor_tensor(gt[0:1, 1:2], gt[0:1, 1:2], gt[0:1, 0:1], op=ALU.subtract), sem=sg)
    ch()
    P.emit(V, lambda e: e.tensor_scalar(gt[0:1, 2:3], c[0:1, 3:4], -1.0, 4.0, op0=ALU.mult, op1=ALU.add), sem=sg)
    ch()
    P.emit(V, lambda e: e.tensor_tensor(gt[0:1, 3:4], gt[0:1, 1:2], gt[0:1, 2:3], op=ALU.subtract), sem=sg)
    ch()
    P.emit(V, lambda e: e.tensor_tensor(gt[0:1, 3:4], gt[0:1, 3:4], c[0:1, 2:3], op=ALU.mult), sem=sg)
    ch()
    P.emit(V, lambda e: e.tensor_tensor(gt[0:1, 4:5], gt[0:1, 3:4], gt[0:1, 2:3], op=ALU.add), sem=sg)
    ch()
    P.emit(V, lambda e: e.tensor_scalar(c[0:1, 1:2], gt[0:1, 4:5], 3.0, None, op0=ALU.is_equal), sem=sg)
    ch()
    P.emit(V, lambda e: e.tensor_scalar(c[0:1, 2:3], gt[0:1, 4:5], 4.0, None, op0=ALU.is_equal), sem=sg)
    ch()
    P.emit(V, lambda e: e.tensor_scalar(c[0:1, 3:4], gt[0:1, 4:5], 1.5, None, op0=ALU.is_gt), sem=sg)
    ch()
    P.emit(V, lambda e: e.tensor_tensor(gt[0:1, 5:6], c[0:1, 1:2], c[0:1, 2:3], op=ALU.add), sem=sg)
    ch()
    P.emit(V, lambda e: e.tensor_scalar(c[0:1, 0:1], gt[0:1, 5:6], -1.0, 1.0, op0=ALU.mult, op1=ALU.add), sem=sg)
    ch()
    P.wait("tensor", sg, P.cnt(sg))
    P.emit("tensor", lambda e: e.matmul(gsb["pm"][:, :], pre["ones_row"][0:1, :],
                                        c[0:1, :], start=True, stop=True), sem=sgm)
    P.wait(V, sgm, P.cnt(sgm))
    P.emit(V, lambda e: e.tensor_copy(gsb["mcol"][:, :], gsb["pm"][:, :]), sem=sgm)
    return sgm


def emit_ffn1(P, tag, x_d, xout_d, w1t_d, aall_sb, ball_sb, bcol_sb, gsb, pre, sb,
              y_d=None, lngate=None):
    nc = P.nc
    with nc.psum_tensor(tag + "ps1", [1, TC], F32) as ps1, \
         nc.psum_tensor(tag + "ps2", [1, TC], F32) as ps2, \
         nc.psum_tensor(tag + "psr", [128, TC], F32) as psr:
        lnrep = emit_ln(P, tag + "N", x_d, pre, sb, {"s1": ps1, "s2": ps2, "rep": psr},
                        y_d=y_d)
    shb = P.sem(tag + "hb")
    P.emit("vector", lambda e: e.tensor_scalar(
        sb["hb"][:, :], bcol_sb[:, :], 0.5, None, op0=ALU.mult), sem=shb)
    sae = P.sem(tag + "ae")
    mc = gsb["mcol"]
    for k, (dst, src, n, msl) in enumerate((
            ("aeff", aall_sb, ND * R, slice(0, 128)),
            ("beff", ball_sb, ND * 128, slice(0, 16)))):
        P.emit("vector", lambda e, dst=dst, src=src, n=n, msl=msl: e.tensor_scalar(
            sb[dst][msl, :], src[msl, 0:n], mc[msl, 0:1], None, op0=ALU.mult), sem=sae)
        P.wait("vector", sae, P.cnt(sae))
        for j in (1, 2):
            P.emit("vector", lambda e, dst=dst, src=src, n=n, j=j, msl=msl: e.scalar_tensor_tensor(
                sb[dst][msl, :], src[msl, j * n:(j + 1) * n], mc[msl, j:j + 1],
                sb[dst][msl, :], op0=ALU.mult, op1=ALU.add), sem=sae)
            P.wait("vector", sae, P.cnt(sae))

    prev_sh3 = None
    prev_sso = None
    prev_sev = None
    prev_sxe = None
    for half in range(2):
        h = str(half)
        xln = sb["xln1"]
        sln = emit_xln_half(P, tag + "H" + h, x_d, sb, half, xln, y_d=y_d,
                            sync_gate=lnrep if half == 0 else prev_sh3,
                            vec_gate=prev_sh3)
        sw = (P.sem(tag + h + "wA"), P.sem(tag + h + "wB"))
        sh3 = P.sem(tag + h + "h3")
        sxr = P.sem(tag + h + "xr")
        sxe = P.sem(tag + h + "xe")
        sev = P.sem(tag + h + "ev")
        sfi = P.sem(tag + h + "fi")
        sxc = P.sem(tag + h + "xc")
        sxo = (P.sem(tag + h + "oA"), P.sem(tag + h + "oB"))
        with nc.psum_tensor(tag + h + "pa", [128, TH], F32) as pa, \
             nc.psum_tensor(tag + h + "pb", [128, TH], F32) as pb, \
             nc.psum_tensor(tag + h + "pxr", [16, TH], F32) as pxr:
            pp = (pa, pb)
            P.wait("tensor", sln, P.cnt(sln))
            P.wait("tensor", sae, P.cnt(sae))
            if prev_sso is not None:
                for s_ in prev_sso:
                    P.wait("tensor", s_, P.cnt(s_))
                P.wait("tensor", prev_sev, P.cnt(prev_sev))
                P.wait("tensor", prev_sxe, P.cnt(prev_sxe))
            for c in range(ND):
                P.emit("tensor", lambda e, c=c: e.matmul(
                    pxr[:, :], sb["aeff"][:, c * R:(c + 1) * R],
                    xln[:, c * TH:(c + 1) * TH],
                    start=(c == 0), stop=(c == ND - 1)), sem=sxr)
            P.wait("vector", sxr, P.cnt(sxr))
            P.wait("vector", shb, 1)
            P.emit("vector", lambda e: e.tensor_scalar(
                sb["xrb1"][0:16, :], pxr[:, :], mc[0:16, 3:4], None,
                op0=ALU.mult), sem=sxe)
            t = slice(half * TH, half * TH + TH)
            for ot in range(ND):
                fp = ot % 2
                wsl = slice(fp * ND * 128, (fp + 1) * ND * 128)
                if ot >= 2:
                    P.wait("gpsimd", sh3, ot - 1)
                elif prev_sh3 is not None:
                    P.wait("gpsimd", prev_sh3, P.cnt(prev_sh3))
                P.dma("gpsimd", sb["w1s1"][:, wsl],
                      w1t_d[ot].rearrange("p c j -> p (c j)"), sw[fp])
                P.wait("tensor", sw[fp], 16 * (ot // 2 + 1))
                if ot == 0:
                    P.wait("tensor", sxe, 1)
                if ot >= 2:
                    P.wait("tensor", sev, ot - 1)
                for c in range(ND):
                    P.emit("tensor", lambda e, fp=fp, wsl=wsl, c=c: e.matmul(
                        pp[fp][:, :], sb["w1s1"][:, wsl][:, c * 128:(c + 1) * 128],
                        xln[:, c * TH:(c + 1) * TH], start=(c == 0), stop=False))
                P.emit("tensor", lambda e, fp=fp, ot=ot: e.matmul(
                    pp[fp][:, :], sb["beff"][:, ot * 128:(ot + 1) * 128],
                    sb["xrb1"][0:16, :], start=False, stop=True), sem=sh3)
                kfi = 2 if y_d is not None else 1
                P.wait("scalar", sh3, P.cnt(sh3))
                if ot == 0:
                    P.wait("scalar", shb, 1)
                if ot >= 2:
                    P.wait("scalar", sfi, kfi * (ot - 1))
                P.emit("scalar", lambda e, fp=fp, ot=ot: e.activation(
                    out=sb["s11"][:, fp * TH:(fp + 1) * TH], in_=pp[fp][:, :],
                    func=AF.Identity, bias=sb["hb"][:, ot:ot + 1],
                    scale=pre["halfc"][:, 0:1]), sem=sev)
                cp = ot % 2
                csl = slice(cp * TH, (cp + 1) * TH)
                if ot == 0:
                    P.wait("sync", sln, P.cnt(sln))
                if ot >= 2:
                    P.wait("sync", sfi, kfi * (ot - 1))
                P.dma("sync", sb["xc1"][:, csl], x_d[ot * 128:(ot + 1) * 128, t], sxc)
                if y_d is not None:
                    P.dma("sync", sb["yc1"][:, csl], y_d[ot * 128:(ot + 1) * 128, t], sxc)
                P.wait("vector", sxc, P.cnt(sxc))
                P.wait("vector", sev, P.cnt(sev))
                if ot >= 2:
                    P.wait("vector", sxo[cp], 16 * (ot // 2))
                if y_d is not None:
                    P.emit("vector", lambda e, cp=cp, csl=csl: e.tensor_tensor(
                        sb["xc1"][:, csl], sb["xc1"][:, csl], sb["yc1"][:, csl],
                        op=ALU.add), sem=sfi)
                    P.wait("vector", sfi, P.cnt(sfi))
                P.emit("vector", lambda e, fp=fp, csl=csl: e.tensor_tensor(
                    sb["xo1"][:, csl], sb["s11"][:, fp * TH:(fp + 1) * TH],
                    sb["xc1"][:, csl], op=ALU.add), sem=sfi)
                P.wait("sync", sfi, P.cnt(sfi))
                P.dma("sync", xout_d[ot * 128:(ot + 1) * 128, t],
                      sb["xo1"][:, csl], sxo[cp])
        prev_sh3 = sh3
        prev_sso = sxo
        prev_sev = sev
        prev_sxe = sxe
        last_sxo = sxo
    return last_sxo


def alloc_ffn1_sbuf(nc, stack, tag):
    S = {}
    def sb(name, shape, dt):
        S[name] = stack.enter_context(nc.sbuf_tensor(tag + name, shape, dt))
        return S[name]
    sb("xln1", [128, ND * TH], BF16)
    sb("w1s1", [128, 2 * ND * 128], BF16)
    sb("aeff", [128, ND * R], BF16)
    sb("beff", [16, ND * 128], BF16)
    sb("xrb1", [16, TH], BF16)
    sb("s11", [128, 2 * TH], F32)
    sb("xc1", [128, 2 * TH], F32)
    sb("yc1", [128, 2 * TH], F32)
    sb("xo1", [128, 2 * TH], F32)
    sb("hb", [128, ND], F32)
    sb("ab", [128, 2 * TC], F32)
    sb("row", [1, 4 * TC], F32)
    S["xb"] = S["xc1"]
    S["sq"] = S["xo1"]
    S["xln"] = S["xln1"]
    return S


def build_l4(with_y=True):
    nc = bass.Bass("TRN2", target_bir_lowering=False, debug=False)
    x = nc.dram_tensor("x", [D, TC], F32, kind="ExternalInput").ap()
    y = nc.dram_tensor("y", [D, TC], F32, kind="ExternalInput").ap() if with_y else None
    w1t = nc.dram_tensor("fw1t", [ND, 128, ND, 128], F32, kind="ExternalInput").ap()
    aall = nc.dram_tensor("aall", [3, 128, ND, R], F32, kind="ExternalInput").ap()
    ball = nc.dram_tensor("ball", [3, R, ND, 128], F32, kind="ExternalInput").ap()
    bcol = nc.dram_tensor("bcol", [128, ND], F32, kind="ExternalInput").ap()
    lsi = nc.dram_tensor("lsi", [1, 4], I32, kind="ExternalInput").ap()
    xout = nc.dram_tensor("xout", [D, TC], F32, kind="ExternalOutput").ap()

    with ExitStack() as st:
        P = Prog(nc)
        sb = alloc_ffn1_sbuf(nc, st, "f")
        aall_sb = st.enter_context(nc.sbuf_tensor("aall_sb", [128, 3 * ND * R], BF16))
        ball_sb = st.enter_context(nc.sbuf_tensor("ball_sb", [16, 3 * ND * 128], BF16))
        bcol_sb = st.enter_context(nc.sbuf_tensor("bcol_sb", [128, ND], F32))
        lsi_sb = st.enter_context(nc.sbuf_tensor("lsi_sb", [1, 4], F32))
        gsb = {
            "gt": st.enter_context(nc.sbuf_tensor("g_gt", [1, 8], F32)),
            "cvec": st.enter_context(nc.sbuf_tensor("g_cv", [1, 4], F32)),
            "mcol": st.enter_context(nc.sbuf_tensor("g_mc", [128, 4], F32)),
        }
        pre = make_pre(nc, st, P)
        sc0 = P.sem("c0")
        for j in range(3):
            P.dma("gpsimd", aall_sb[:, j * ND * R:(j + 1) * ND * R],
                  aall[j].rearrange("p c r -> p (c r)"), sc0)
            P.dma("gpsimd", ball_sb[:, j * ND * 128:(j + 1) * ND * 128],
                  ball[j].rearrange("r c j2 -> r (c j2)"), sc0)
        P.dma("gpsimd", bcol_sb[:, :], bcol, sc0)
        P.dma("gpsimd", lsi_sb[:, :], lsi, sc0)
        P.wait("vector", sc0, P.cnt(sc0))
        with nc.psum_tensor("g_pm", [128, 4], F32) as gpm:
            gsb["pm"] = gpm
            emit_gate_masks(P, "G", lsi_sb, pre, gsb)
            sxo = emit_ffn1(P, "F1", x, xout, w1t, aall_sb, ball_sb, bcol_sb,
                            gsb, pre, sb, y_d=y)
        for s_ in sxo:
            P.wait("sync", s_, P.cnt(s_))
        for nm in P.counts:
            if nm not in P.handles:
                P.handles[nm] = st.enter_context(nc.semaphore(nm))
        with nc.Block() as block:
            P.build_block(block)
    return nc


NCH = T // TH


def emit_mamba(P, tag, xm_d, xpf_d, dtb_sb, alog_sb, dcol_sb, sel_sb, y_d, pre, sb):
    nc = P.nc
    sdt = P.sem(tag + "dt")
    sdm = P.sem(tag + "dm")
    sdz = P.sem(tag + "dz")
    sdv = P.sem(tag + "dv")
    sA = P.sem(tag + "A")
    sbc = (P.sem(tag + "bcA"), P.sem(tag + "bcB"))
    sxc = (P.sem(tag + "xcA"), P.sem(tag + "xcB"))
    sbr = P.sem(tag + "br")
    scr = P.sem(tag + "cr")
    sda = P.sem(tag + "da")
    sdb = P.sem(tag + "db")
    ssc = P.sem(tag + "sc")
    sst = P.sem(tag + "st")
    spr = P.sem(tag + "pr")
    srd = P.sem(tag + "rd")
    sy = P.sem(tag + "y")
    syo = (P.sem(tag + "yoA"), P.sem(tag + "yoB"))

    dtt, dtin, dA, dBx, hb = sb["dtt"], sb["dtin"], sb["dA"], sb["dBx"], sb["hh"]
    bcB, bcC, ub, state = sb["bcB"], sb["bcC"], sb["ub"], sb["state"]
    xcm, yo, zt, an = sb["xcm"], sb["yo"], sb["zt"], sb["an"]

    P.dma("gpsimd", dtin[:, :], xpf_d[0:DTRANK, :], sdt)
    P.emit("scalar", lambda e: e.activation(out=an[:, :], in_=alog_sb[:, :],
                                            func=AF.Exp), sem=sA)
    P.wait("vector", sA, 1)
    P.emit("vector", lambda e: e.tensor_scalar(an[:, :], an[:, :], -1.0, None,
                                               op0=ALU.mult), sem=sA)

    with nc.psum_tensor(tag + "pdt", [128, TH], F32) as pdt, \
         nc.psum_tensor(tag + "pB", [128, 2 * TH], F32) as pB, \
         nc.psum_tensor(tag + "pC", [128, 2 * TH], F32) as pC:
        for dd in range(NDL):
            if dd == 0:
                P.wait("tensor", sdt, 16)
            for ch in range(NCH):
                P.wait("tensor", sdz, P.cnt(sdz))
                if ch == 0 and dd > 0:
                    P.wait("tensor", sda, 16 * NCH * dd)
                P.emit("tensor", lambda e, dd=dd, ch=ch: e.matmul(
                    pdt[:, :], sb["dtpt"][:, dd * 128:(dd + 1) * 128],
                    dtin[:, ch * TH:(ch + 1) * TH], start=True, stop=True), sem=sdm)
                P.wait("scalar", sdm, P.cnt(sdm))
                zsl = slice((ch % 2) * TH, (ch % 2) * TH + TH)
                z2 = slice(2 * TC + zsl.start, 2 * TC + zsl.stop)
                z1 = slice(TC + zsl.start, TC + zsl.stop)
                if dd * NCH + ch >= 2:
                    P.wait("scalar", sdv, dd * NCH + ch - 1)
                P.emit("scalar", lambda e, zsl=zsl, dd=dd: e.activation(
                    out=zt[:, zsl], in_=pdt[:, :], func=AF.Identity,
                    bias=dtb_sb[:, dd:dd + 1]), sem=sdz)
                P.emit("scalar", lambda e, zsl=zsl, z2=z2: e.activation(
                    out=zt[:, z2], in_=zt[:, zsl], func=AF.Abs), sem=sdz)
                P.wait("scalar", sdz, P.cnt(sdz))
                P.emit("scalar", lambda e, z2=z2: e.activation(
                    out=zt[:, z2], in_=zt[:, z2], func=AF.Exp,
                    scale=pre["negc"][:, 0:1]), sem=sdz)
                P.wait("scalar", sdz, P.cnt(sdz))
                P.emit("scalar", lambda e, z2=z2: e.activation(
                    out=zt[:, z2], in_=zt[:, z2], func=AF.Ln,
                    bias=pre["ones_col"][:, 0:1]), sem=sdz)
                P.wait("scalar", sdz, P.cnt(sdz))
                P.emit("scalar", lambda e, zsl=zsl, z1=z1: e.activation(
                    out=zt[:, z1], in_=zt[:, zsl], func=AF.Relu), sem=sdz)
                P.wait("vector", sdz, P.cnt(sdz))
                P.emit("vector", lambda e, z1=z1, z2=z2, ch=ch: e.tensor_tensor(
                    dtt[:, ch * TH:(ch + 1) * TH], zt[:, z1], zt[:, z2],
                    op=ALU.add), sem=sdv)

            P.wait("vector", sA, 2)
            P.wait("vector", sdv, P.cnt(sdv))
            if dd > 0:
                P.wait("vector", srd, NCH * dd)
            P.emit("vector", lambda e: e.memset(state[:, :], 0.0), sem=sst)
            for ch in range(NCH):
                k = dd * NCH + ch
                cp = ch % 2
                if k >= 2:
                    P.wait("sync", sbr, 16 * (k - 1))
                    P.wait("sync", scr, 16 * (k - 1))
                P.dma("sync", bcB[:, cp * TH:(cp + 1) * TH],
                      xpf_d[DTRANK:DTRANK + 16, ch * TH:(ch + 1) * TH], sbc[cp])
                P.dma("sync", bcC[:, cp * TH:(cp + 1) * TH],
                      xpf_d[DTRANK + 16:DTRANK + 32, ch * TH:(ch + 1) * TH], sbc[cp])
                if k >= 2:
                    P.wait("sync", sy, k - 1)
                P.dma("sync", xcm[:, cp * TH:(cp + 1) * TH],
                      xm_d[dd * 128:(dd + 1) * 128, ch * TH:(ch + 1) * TH], sxc[cp])
                P.wait("vector", sxc[cp], 16 * (k // 2 + 1))
                P.wait("vector", ssc, P.cnt(ssc))
                P.emit("vector", lambda e, ch=ch, cp=cp: e.tensor_tensor(
                    ub[:, cp * TH:(cp + 1) * TH], dtt[:, ch * TH:(ch + 1) * TH],
                    xcm[:, cp * TH:(cp + 1) * TH], op=ALU.mult), sem=sdb)
                P.wait("scalar", srd, k)
                for seg in range(16):
                    sp = seg % 2
                    P.wait("tensor", sbc[cp], 32 * (k // 2 + 1))
                    if seg >= 2:
                        P.wait("tensor", sdb, 17 * k + seg)
                    elif k > 0:
                        P.wait("tensor", sdb, 17 * k)
                    P.emit("tensor", lambda e, sp=sp, cp=cp, seg=seg: e.matmul(
                        pB[:, sp * TH:(sp + 1) * TH],
                        sel_sb[:, seg * 128:(seg + 1) * 128],
                        bcB[:, cp * TH:(cp + 1) * TH], start=True, stop=True), sem=sbr)
                    P.emit("scalar", lambda e, seg=seg, ch=ch, dd=dd: e.activation(
                        out=dA[:, seg * TH:(seg + 1) * TH],
                        in_=dtt[:, ch * TH:(ch + 1) * TH], func=AF.Exp,
                        scale=an[:, dd * 16 + seg:dd * 16 + seg + 1]), sem=sda)
                    P.wait("vector", sbr, P.cnt(sbr))
                    P.emit("vector", lambda e, seg=seg, cp=cp, sp=sp: e.tensor_tensor(
                        dBx[:, seg * TH:(seg + 1) * TH], ub[:, cp * TH:(cp + 1) * TH],
                        pB[:, sp * TH:(sp + 1) * TH], op=ALU.mult), sem=sdb)
                P.wait("vector", sda, P.cnt(sda))
                P.wait("vector", sdb, P.cnt(sdb))
                P.wait("vector", sst, P.cnt(sst))
                for seg in range(16):
                    P.emit("vector", lambda e, seg=seg: e.tensor_tensor_scan(
                        hb[:, seg * TH:(seg + 1) * TH], dA[:, seg * TH:(seg + 1) * TH],
                        dBx[:, seg * TH:(seg + 1) * TH], state[:, seg:seg + 1],
                        op0=ALU.mult, op1=ALU.add), sem=ssc)
                P.wait("vector", ssc, P.cnt(ssc))
                P.emit("vector", lambda e: e.tensor_copy(
                    state[:, 0:16].rearrange("p (s o) -> p s o", o=1),
                    hb.rearrange("p (s t) -> p s t", t=TH)[:, :, TH - 1:TH]), sem=sst)
                P.wait("vector", srd, k)
                for seg in range(16):
                    sp = seg % 2
                    if seg >= 2:
                        P.wait("tensor", spr, 16 * k + seg - 1)
                    elif k > 0:
                        P.wait("tensor", spr, 16 * k)
                    P.emit("tensor", lambda e, sp=sp, cp=cp, seg=seg: e.matmul(
                        pC[:, sp * TH:(sp + 1) * TH],
                        sel_sb[:, seg * 128:(seg + 1) * 128],
                        bcC[:, cp * TH:(cp + 1) * TH], start=True, stop=True), sem=scr)
                    P.wait("vector", scr, P.cnt(scr))
                    P.emit("vector", lambda e, seg=seg, sp=sp: e.tensor_tensor(
                        dA[:, seg * TH:(seg + 1) * TH], hb[:, seg * TH:(seg + 1) * TH],
                        pC[:, sp * TH:(sp + 1) * TH], op=ALU.mult), sem=spr)
                P.wait("vector", spr, P.cnt(spr))
                if k > 0:
                    P.wait("vector", sy, k)
                P.emit("vector", lambda e, cp=cp: e.tensor_reduce(
                    yo[:, cp * TH:(cp + 1) * TH],
                    dA.rearrange("p (s t) -> p t s", t=TH), axis=AX.X, op=ALU.add),
                    sem=srd)
                P.wait("vector", srd, P.cnt(srd))
                if k >= 2:
                    P.wait("vector", syo[cp], 16 * (k // 2))
                P.emit("vector", lambda e, cp=cp, dd=dd: e.scalar_tensor_tensor(
                    yo[:, 2 * TH + cp * TH:2 * TH + (cp + 1) * TH],
                    xcm[:, cp * TH:(cp + 1) * TH], dcol_sb[:, dd:dd + 1],
                    yo[:, cp * TH:(cp + 1) * TH], op0=ALU.mult, op1=ALU.add), sem=sy)
                P.wait("sync", sy, P.cnt(sy))
                P.dma("sync", y_d[dd * 128:(dd + 1) * 128, ch * TH:(ch + 1) * TH],
                      yo[:, 2 * TH + cp * TH:2 * TH + (cp + 1) * TH], syo[cp])
    return syo


def alloc_mamba_sbuf(nc, stack, tag):
    S = {}
    def sb(name, shape, dt):
        S[name] = stack.enter_context(nc.sbuf_tensor(tag + name, shape, dt))
        return S[name]
    sb("dtt", [128, NCH * TH], F32)
    sb("dtin", [128, T], BF16)
    sb("dA", [128, 16 * TH], F32)
    sb("dBx", [128, 16 * TH], F32)
    sb("hh", [128, 16 * TH], F32)
    sb("bcB", [16, 2 * TH], F32)
    sb("bcC", [16, 2 * TH], F32)
    sb("ub", [128, 2 * TH], F32)
    sb("state", [128, 16], F32)
    sb("xcm", [128, 2 * TH], F32)
    sb("yo", [128, 4 * TH], F32)
    sb("zt", [128, 3 * TC], F32)
    sb("an", [128, NDL * 16], F32)
    sb("dtpt", [128, NDL * 128], BF16)
    return S


def build_l3():
    nc = bass.Bass("TRN2", target_bir_lowering=False, debug=False)
    xm = nc.dram_tensor("xm", [DLOC, T], F32, kind="ExternalInput").ap()
    xpf = nc.dram_tensor("xpf", [160, T], F32, kind="ExternalInput").ap()
    dtpt = nc.dram_tensor("dtpt", [128, NDL * 128], F32, kind="ExternalInput").ap()
    dtb = nc.dram_tensor("dtb", [128, NDL], F32, kind="ExternalInput").ap()
    alog = nc.dram_tensor("alog", [128, NDL * 16], F32, kind="ExternalInput").ap()
    dcol = nc.dram_tensor("dcol", [128, NDL], F32, kind="ExternalInput").ap()
    sel = nc.dram_tensor("sel", [16, 16 * 128], F32, kind="ExternalInput").ap()
    y = nc.dram_tensor("y", [DLOC, T], F32, kind="ExternalOutput").ap()
    with ExitStack() as st:
        P = Prog(nc)
        sb = alloc_mamba_sbuf(nc, st, "m")
        dtb_sb = st.enter_context(nc.sbuf_tensor("dtb_sb", [128, NDL], F32))
        alog_sb = st.enter_context(nc.sbuf_tensor("alog_sb", [128, NDL * 16], F32))
        dcol_sb = st.enter_context(nc.sbuf_tensor("dcol_sb", [128, NDL], F32))
        sel_sb = st.enter_context(nc.sbuf_tensor("sel_sb", [16, 16 * 128], F32))
        pre = make_pre(nc, st, P)
        sc0 = P.sem("c0")
        P.dma("gpsimd", dtb_sb[:, :], dtb, sc0)
        P.dma("gpsimd", alog_sb[:, :], alog, sc0)
        P.dma("gpsimd", dcol_sb[:, :], dcol, sc0)
        P.dma("gpsimd", sel_sb[:, :], sel, sc0)
        P.dma("gpsimd", sb["dtpt"][:, :], dtpt, sc0)
        P.wait("scalar", sc0, P.cnt(sc0))
        P.wait("tensor", sc0, P.cnt(sc0))
        P.wait("vector", sc0, P.cnt(sc0))
        syo = emit_mamba(P, "M", xm, xpf, dtb_sb, alog_sb, dcol_sb, sel_sb, y, pre, sb)
        for s_ in syo:
            P.wait("sync", s_, P.cnt(s_))
        for nm in P.counts:
            if nm not in P.handles:
                P.handles[nm] = st.enter_context(nc.semaphore(nm))
        with nc.Block() as block:
            P.build_block(block)
    return nc


def emit_xln_out(P, tag, x_d, sb, xln_out_d, pre):
    sx, sl, so = P.sem(tag + "X"), P.sem(tag + "L"), P.sem(tag + "O")
    xb, sq, ab = sb["xb"], sb["sq"], sb["ab"]
    for i in range(ND):
        if i > 0:
            P.wait("sync", sl, P.cnt(sl))
        P.dma("sync", xb[:, 0:TC], x_d[i * 128:(i + 1) * 128, :], sx)
        P.wait("vector", sx, P.cnt(sx))
        P.emit("vector", lambda e: e.tensor_tensor(
            sq[:, 0:TC], xb[:, 0:TC], ab[:, 0:TC], op=ALU.mult), sem=sl)
        P.wait("vector", sl, P.cnt(sl))
        if i >= 1:
            P.wait("vector", so, 16 * i)
        P.emit("vector", lambda e: e.tensor_tensor(
            xb[:, 0:TC], sq[:, 0:TC], ab[:, TC:2 * TC], op=ALU.subtract), sem=sl)
        P.wait("sync", sl, P.cnt(sl))
        P.dma("sync", xln_out_d[i * 128:(i + 1) * 128, :], xb[:, 0:TC], so)
    return so


def build_l2c():
    nc = bass.Bass("TRN2", target_bir_lowering=False, debug=False)
    x = nc.dram_tensor("x", [D, TC], F32, kind="ExternalInput").ap()
    w1t = nc.dram_tensor("fw1t", [ND, 128, ND, 128], F32, kind="ExternalInput").ap()
    aall = nc.dram_tensor("aall", [3, 128, ND, R], F32, kind="ExternalInput").ap()
    ball = nc.dram_tensor("ball", [3, R, ND, 128], F32, kind="ExternalInput").ap()
    bcol = nc.dram_tensor("bcol", [128, ND], F32, kind="ExternalInput").ap()
    lsi = nc.dram_tensor("lsi", [1, 4], I32, kind="ExternalInput").ap()
    wxpt = nc.dram_tensor("wxpt", [2, 128, ND, 128], F32, kind="ExternalInput").ap()
    xout = nc.dram_tensor("xout", [D, TC], F32, kind="ExternalOutput").ap()
    xlno = nc.dram_tensor("xlno", [D, TC], F32, kind="ExternalOutput").ap()
    xpo = nc.dram_tensor("xpo", [256, TC], F32, kind="ExternalOutput").ap()

    with ExitStack() as st:
        P = Prog(nc)
        sb = alloc_ffn1_sbuf(nc, st, "f")
        sb["s1b"] = st.enter_context(nc.sbuf_tensor("s1bf", [128, 2 * TH], F32))
        sb["w1s"] = sb["w1s1"]
        aall_sb = st.enter_context(nc.sbuf_tensor("aall_sb", [128, 3 * ND * R], BF16))
        ball_sb = st.enter_context(nc.sbuf_tensor("ball_sb", [16, 3 * ND * 128], BF16))
        bcol_sb = st.enter_context(nc.sbuf_tensor("bcol_sb", [128, ND], F32))
        lsi_sb = st.enter_context(nc.sbuf_tensor("lsi_sb", [1, 4], F32))
        gsb = {
            "gt": st.enter_context(nc.sbuf_tensor("g_gt", [1, 8], F32)),
            "cvec": st.enter_context(nc.sbuf_tensor("g_cv", [1, 4], F32)),
            "mcol": st.enter_context(nc.sbuf_tensor("g_mc", [128, 4], F32)),
        }
        pre = make_pre(nc, st, P)
        sc0 = P.sem("c0")
        for j in range(3):
            P.dma("gpsimd", aall_sb[:, j * ND * R:(j + 1) * ND * R],
                  aall[j].rearrange("p c r -> p (c r)"), sc0)
            P.dma("gpsimd", ball_sb[:, j * ND * 128:(j + 1) * ND * 128],
                  ball[j].rearrange("r c j2 -> r (c j2)"), sc0)
        P.dma("gpsimd", bcol_sb[:, :], bcol, sc0)
        P.dma("gpsimd", lsi_sb[:, :], lsi, sc0)
        P.wait("vector", sc0, P.cnt(sc0))
        with nc.psum_tensor("g_pm", [128, 4], F32) as gpm:
            gsb["pm"] = gpm
            emit_gate_masks(P, "G", lsi_sb, pre, gsb)
            sxo = emit_ffn1(P, "F1", x, xout, w1t, aall_sb, ball_sb, bcol_sb,
                            gsb, pre, sb)
        with nc.psum_tensor("qs1", [1, TC], F32) as ps1, \
             nc.psum_tensor("qs2", [1, TC], F32) as ps2, \
             nc.psum_tensor("qsr", [128, TC], F32) as psr:
            for s_ in sxo:
                P.wait("sync", s_, P.cnt(s_))
            qrep = emit_ln(P, "Q", xout, pre, sb, {"s1": ps1, "s2": ps2, "rep": psr})
        so = emit_xln_out(P, "XO", xout, sb, xlno, pre)
        emit_proj(P, "XP", xout, wxpt, None, 2, xpo, pre, sb, gate=qrep)
        P.wait("sync", so, P.cnt(so))
        for nm in P.counts:
            if nm not in P.handles:
                P.handles[nm] = st.enter_context(nc.semaphore(nm))
        with nc.Block() as block:
            P.build_block(block)
    return nc



_BUILDS = {}


def _get_build(name):
    if name not in _BUILDS:
        _BUILDS[name] = {
            "l1": lambda: build_l1(),
            "l1b": lambda: build_l1(skip_qkv=True),
            "l2": lambda: build_l2_attn_only(),
            "l2c": lambda: build_l2c(),
            "l3": lambda: build_l3(),
            "l4": lambda: build_l4(with_y=True),
        }[name]()
    return _BUILDS[name]


def kernel(L0_attn_x, L0_attn_Wqk_w, L0_attn_Wqk_b, L0_attn_Wv_w, L0_attn_Wv_b,
           L0_attn_Wo_w, L0_attn_Wo_b, L0_ffn_w1, L0_ffn_w3, L0_ffn_w2,
           L0_ffn_lora_A1, L0_ffn_lora_B1, L0_ffn_lora_A3, L0_ffn_lora_B3,
           L0_ffn_lora_A2, L0_ffn_lora_B2, L1_attn_A_log, L1_attn_D,
           L1_attn_x_proj_w, L1_attn_dt_proj_w, L1_attn_dt_proj_b,
           L1_ffn_weight, L1_ffn_bias, L1_ffn_lora_A, L1_ffn_lora_B,
           L1_ffn_lora_start_indices):
    f32 = np.float32
    x_full = np.asarray(L0_attn_x, f32)[0]
    cores = list(range(NCORES))

    ffn0_w = {
        "w1t": tile_lhsT(np.asarray(L0_ffn_w1, f32)),
        "w3t": tile_lhsT(np.asarray(L0_ffn_w3, f32)),
        "w2t": tile_lhsT(np.asarray(L0_ffn_w2, f32)),
        "a1t": tile_A(np.asarray(L0_ffn_lora_A1, f32)),
        "a3t": tile_A(np.asarray(L0_ffn_lora_A3, f32)),
        "a2t": tile_A(np.asarray(L0_ffn_lora_A2, f32)),
        "b1t": tile_B(np.asarray(L0_ffn_lora_B1, f32)),
        "b3t": tile_B(np.asarray(L0_ffn_lora_B3, f32)),
        "b2t": tile_B(np.asarray(L0_ffn_lora_B2, f32)),
    }
    hh = np.arange(NHEAD).repeat(EH)
    ee = np.tile(np.arange(EH), NHEAD)
    perm_q = hh * 128 + 2 * ee
    perm_k = perm_q + 1
    wqk = np.asarray(L0_attn_Wqk_w, f32)
    bqk = np.asarray(L0_attn_Wqk_b, f32)
    wqkv = np.concatenate([wqk[perm_q], wqk[perm_k], np.asarray(L0_attn_Wv_w, f32)], 0)
    bqkv_vec = np.concatenate([bqk[perm_q], bqk[perm_k], np.asarray(L0_attn_Wv_b, f32)], 0)
    wqkvt = tile_lhsT(wqkv)
    bqkv = np.ascontiguousarray(bqkv_vec.reshape(12, 128).T)
    wot = tile_lhsT(np.asarray(L0_attn_Wo_w, f32))
    wob = np.ascontiguousarray(np.asarray(L0_attn_Wo_b, f32).reshape(ND, 128).T)
    idb = np.eye(128, dtype=f32).astype(bfdt)
    idf = np.eye(128, dtype=f32)

    fw1t = tile_lhsT(np.asarray(L1_ffn_weight, f32))
    aall = np.stack([tile_A(np.asarray(L1_ffn_lora_A, f32)[j]) for j in range(3)])
    ball = np.stack([tile_B(np.asarray(L1_ffn_lora_B, f32)[j]) for j in range(3)])
    bcol = np.ascontiguousarray(np.asarray(L1_ffn_bias, f32).reshape(ND, 128).T)
    lsi = np.ascontiguousarray(np.asarray(L1_ffn_lora_start_indices, np.int32).reshape(1, 4))

    xpw = np.asarray(L1_attn_x_proj_w, f32)
    xpw_pad = np.concatenate([xpw, np.zeros((256 - xpw.shape[0], D), f32)], 0)
    wxpt = tile_lhsT(xpw_pad)
    dtw = np.asarray(L1_attn_dt_proj_w, f32)
    dtbv = np.asarray(L1_attn_dt_proj_b, f32)
    alogv = np.asarray(L1_attn_A_log, f32)
    dv = np.asarray(L1_attn_D, f32)
    sel = np.ascontiguousarray(
        (np.eye(16, dtype=f32)[:, :, None] * np.ones(128, f32)).reshape(16, 16 * 128))

    maps = []
    for c in cores:
        m = dict(ffn0_w)
        m["x"] = np.ascontiguousarray(x_full[c * TC:(c + 1) * TC].T)
        m["wqkvt"] = wqkvt
        m["bqkv"] = bqkv
        maps.append(m)
    r1 = run_bass_kernel_spmd(_get_build("l1"), maps, core_ids=cores).results
    x1 = [r1[c]["xout"] for c in cores]
    qkv = [r1[c]["qkv"] for c in cores]

    kT = np.concatenate([q[512:1024] for q in qkv], 1)
    vT = np.concatenate([q[1024:1536] for q in qkv], 1)

    maps = [{"x": x1[c], "qT": np.ascontiguousarray(qkv[c][0:512]),
             "kT": kT, "vT": vT, "wot": wot, "wob": wob, "idb": idb, "idf": idf}
            for c in cores]
    r2 = run_bass_kernel_spmd(_get_build("l2"), maps, core_ids=cores).results
    x2 = [r2[c]["x2"] for c in cores]

    maps = []
    for c in cores:
        m = dict(ffn0_w)
        m["x"] = x2[c]
        maps.append(m)
    r3 = run_bass_kernel_spmd(_get_build("l1b"), maps, core_ids=cores).results
    x3 = [r3[c]["xout"] for c in cores]

    maps = [{"x": x3[c], "fw1t": fw1t, "aall": aall, "ball": ball, "bcol": bcol,
             "lsi": lsi, "wxpt": wxpt} for c in cores]
    r4 = run_bass_kernel_spmd(_get_build("l2c"), maps, core_ids=cores).results
    x4 = [r4[c]["xout"] for c in cores]
    xln5 = np.concatenate([r4[c]["xlno"] for c in cores], 1)
    xpf = np.concatenate([r4[c]["xpo"][0:160] for c in cores], 1)

    maps = []
    for c in cores:
        c0 = c * DLOC
        maps.append({
            "xm": np.ascontiguousarray(xln5[c0:c0 + DLOC]),
            "xpf": xpf,
            "dtpt": np.ascontiguousarray(dtw.T[:, c0:c0 + DLOC].reshape(128, NDL * 128)),
            "dtb": np.ascontiguousarray(dtbv[c0:c0 + DLOC].reshape(NDL, 128).T),
            "alog": np.ascontiguousarray(
                alogv[c0:c0 + DLOC].reshape(NDL, 128, 16).transpose(1, 0, 2).reshape(128, NDL * 16)),
            "dcol": np.ascontiguousarray(dv[c0:c0 + DLOC].reshape(NDL, 128).T),
            "sel": sel,
        })
    r5 = run_bass_kernel_spmd(_get_build("l3"), maps, core_ids=cores).results
    y_full = np.concatenate([r5[c]["y"] for c in cores], 0)

    maps = [{"x": x4[c], "y": np.ascontiguousarray(y_full[:, c * TC:(c + 1) * TC]),
             "fw1t": fw1t, "aall": aall, "ball": ball, "bcol": bcol, "lsi": lsi}
            for c in cores]
    r6 = run_bass_kernel_spmd(_get_build("l4"), maps, core_ids=cores).results
    out = np.concatenate([r6[c]["xout"].T for c in cores], 0)
    return np.ascontiguousarray(out[None, :, :]).astype(np.float32)
